# version 41
# baseline (speedup 1.0000x reference)
import numpy as np
from contextlib import ExitStack

import concourse.bass as bass
import concourse.mybir as mybir
from concourse.bass_utils import run_bass_kernel_spmd

F32 = mybir.dt.float32
BF16 = mybir.dt.bfloat16
AF = mybir.ActivationFunctionType
ALU = mybir.AluOpType
AX = mybir.AxisListType

S = 2048
D = 2048
H = 16
DH = 128
NI = 5632
DC = 16
IC = 44
NSUP = 11
SCALE = 128.0 ** -0.25
EPS = 1e-5

STAGE = "full"


class Sem:
    def __init__(self, h):
        self.h = h
        self.n = 0

    def inc(self, k=1):
        self.n += k
        return self.n


def build_nc(stage=STAGE):
    nc = bass.Bass("TRN2", target_bir_lowering=False)

    def din(name, shape, dtype=F32):
        return nc.dram_tensor(name, shape, dtype, kind="ExternalInput").ap()

    x_tm = din("x_tm", [S, D])
    x_fm = din("x_fm", [D, S])
    wqT = din("wqT", [D, D])
    wkT = din("wkT", [D, D])
    wvT = din("wvT", [D, D])
    woT = din("woT", [D, D])
    gwT = din("gwT", [D, NI])
    uwT = din("uwT", [D, NI])
    dwT = din("dwT", [NI, D])
    bq_col = din("bq_col", [128, H])
    bk_col = din("bk_col", [128, H])
    bv_row = din("bv_row", [1, D])
    bo_row = din("bo_row", [1, D])
    ds_row = din("ds_row", [1, D])
    gs_col = din("gs_col", [128, IC])
    us_col = din("us_col", [128, IC])

    out = nc.dram_tensor("out", [S, D], F32, kind="ExternalOutput").ap()
    v_scr = nc.dram_tensor("v_scr", [S, D], BF16, kind="Internal").ap()
    ctx_scr = nc.dram_tensor("ctx_scr", [D, S], BF16, kind="Internal").ap()
    dwc = nc.dram_tensor("dwc", [NI, D], BF16, kind="Internal").ap()
    guc = nc.dram_tensor("guc", [D, 2 * NI], BF16, kind="Internal").ap()
    woc = nc.dram_tensor("woc", [D, D], BF16, kind="Internal").ap()

    REG_W = 45056

    with ExitStack() as st:
        ec = st.enter_context
        xn_t = ec(nc.sbuf_tensor("xn", [128, DC * S], BF16))
        rega = ec(nc.sbuf_tensor("rega", [128, REG_W], BF16))
        bc0 = ec(nc.sbuf_tensor("bc0", [128, D], BF16))
        bc1 = ec(nc.sbuf_tensor("bc1", [128, D], BF16))
        rows_bf = ec(nc.sbuf_tensor("rows_bf", [1, 8192], BF16))
        rows_f = ec(nc.sbuf_tensor("rows_f", [1, 2048], F32))
        ktmp = ec(nc.sbuf_tensor("ktmp", [1, 16], F32))
        negc = [ec(nc.sbuf_tensor(f"negc{i}", [128, 1], F32)) for i in range(2)]
        rs_sb = [ec(nc.sbuf_tensor(f"rs_sb{i}", [128, 512], F32)) for i in range(2)]
        x_t = [ec(nc.sbuf_tensor(f"x_t{i}", [128, 512], F32)) for i in range(4)]
        s1_t = [ec(nc.sbuf_tensor(f"s1_t{i}", [128, 512], F32)) for i in range(2)]
        out_t = [ec(nc.sbuf_tensor(f"out_t{i}", [128, 512], F32)) for i in range(2)]
        mask = ec(nc.sbuf_tensor("mask", [128, 896], BF16))
        ones_col = ec(nc.sbuf_tensor("ones_col", [128, 1], BF16))
        ones128 = ec(nc.sbuf_tensor("ones128", [128, 128], BF16))
        ones_row = ec(nc.sbuf_tensor("ones_row", [1, 128], BF16))
        halfneg_row = ec(nc.sbuf_tensor("halfneg_row", [1, 128], F32))
        bq_s = ec(nc.sbuf_tensor("bq_s", [128, H], F32))
        bk_s = ec(nc.sbuf_tensor("bk_s", [128, H], F32))
        bqs = ec(nc.sbuf_tensor("bqs", [128, H], F32))
        bks = ec(nc.sbuf_tensor("bks", [128, H], F32))
        gs_s = ec(nc.sbuf_tensor("gs_s", [128, IC], F32))
        us_s = ec(nc.sbuf_tensor("us_s", [128, IC], F32))
        cns = ec(nc.sbuf_tensor("cns", [1, 4], F32))
        bank = [ec(nc.psum_tensor(f"bank{i}", [128, 512], F32)) for i in range(8)]

        semnames = ["cPE", "cACT", "cDVE", "cPOOL",
                    "dXF", "dWVb", "dCOL","dWV00", "dWV01", "dWV02", "dWV03", "dWV04", "dWV05", "dWV06", "dWV07", "dWV08", "dWV09", "dWV10", "dWV11", "dWV12", "dWV13", "dWV14", "dWV15", 
                    "dWQ0", "dWQ1", "dWQ2", "dWK0", "dWK1", "dWK2", "dVH0", "dVH1", "dVH2",
                    "dGT0", "dGT1", "dGT2", "dGT3", "dGT4", "dGT5",
                    "dUT0", "dUT1", "dUT2", "dUT3",
                    "dDK0", "dDK1", "dDK2", "dDK3",
                    "dWO0", "dWO1", "dWO2", "dWO3", "dCS", "dDI0", "dDI1", "dDO", "dWC",
                    "dXT0", "dXT1", "dXT2", "dXT3", "dOT0", "dOT1", "dVS0", "dVS1", "dVS2",
                    "dCX0", "dCX1", "dDBG"]
        SM = {n: Sem(ec(nc.semaphore(n))) for n in semnames}
        cPE, cACT, cDVE, cPOOL = SM["cPE"], SM["cACT"], SM["cDVE"], SM["cPOOL"]

        Q = {"pe": [], "act": [], "dve": [], "pool": [], "sp": []}
        last_wait = {}

        def em(q, fn):
            Q[q].append(fn)

        def wait(q, sem, val):
            if val <= 0:
                return
            key = (q, sem.h)
            if last_wait.get(key, 0) >= val:
                return
            last_wait[key] = val
            em(q, lambda e, s=sem.h, v=val: e.wait_ge(s, v))

        def dma(q, sem, out_ap, in_ap):
            v = sem.inc(16)
            em(q, lambda e, o=out_ap, i=in_ap, s=sem.h: e.dma_start(out=o, in_=i).then_inc(s, 16))
            return v

        def op(q, sem, fn):
            v = sem.inc(1)
            em(q, lambda e, f=fn, s=sem.h: f(e).then_inc(s, 1))
            return v

        def op_(q, fn):
            em(q, fn)

        bank_rel = [None] * 8

        def pe_take(b):
            if bank_rel[b] is not None:
                sem, v = bank_rel[b]
                wait("pe", sem, v)
                bank_rel[b] = None

        def A(off, n):
            return rega[:, off:off + n]

        op("dve", cDVE, lambda e: e.memset(ones_col[:], 1.0))
        op("dve", cDVE, lambda e: e.memset(ones128[:], 1.0))
        op("dve", cDVE, lambda e: e.memset(ones_row[:], 1.0))
        op("dve", cDVE, lambda e: e.memset(halfneg_row[:], -0.5))
        op("dve", cDVE, lambda e: e.memset(cns[:, 0:1], EPS))
        v_ones = op("dve", cDVE, lambda e: e.memset(cns[:, 1:2], 1e-20))
        wait("pool", cDVE, v_ones)
        v_mask = op("pool", cPOOL, lambda e: e.affine_select(
            mask[:], ones_col[:].broadcast_to([128, 896]), [[1, 896]], ALU.is_ge, 0.0,
            base=-384, channel_multiplier=-1))

        for (dst, src) in [(bq_s, bq_col), (bk_s, bk_col), (gs_s, gs_col), (us_s, us_col)]:
            dma("pool", SM["dCOL"], dst[:], src)
        v_col = SM["dCOL"].n
        wait("dve", SM["dCOL"], v_col)
        op("dve", cDVE, lambda e: e.tensor_scalar_mul(bqs[:], bq_s[:], SCALE))
        v_bqs = op("dve", cDVE, lambda e: e.tensor_scalar_mul(bks[:], bk_s[:], SCALE))

        xf = [A(d * 2048, 2048) for d in range(DC)]
        v_xfl = [0] * DC
        for d in range(DC):
            v_xfl[d] = dma("pool", SM["dWV%02d" % d], xf[d], x_fm[d * 128:(d + 1) * 128, :])

        xn = [xn_t[:, d * S:(d + 1) * S] for d in range(DC)]
        x2b = [A(32768 + i * 2048, 2048) for i in range(2)]

        wait("pe", cDVE, v_ones)
        pe_x2done = [0] * DC
        for d in range(DC):
            q_ = "dve" if d % 2 == 0 else "pool"
            c_ = cDVE if d % 2 == 0 else cPOOL
            wait(q_, SM["dWV%02d" % d], v_xfl[d])
            if d >= 2:
                wait(q_, cPE, pe_x2done[d - 2])
            v_x2 = op(q_, c_, lambda e, d=d: e.tensor_tensor(
                x2b[d % 2][:], xf[d][:], xf[d][:], ALU.mult))
            wait("pe", c_, v_x2)
            if d == 0:
                for b in range(8):
                    pe_take(b)
            for sl in range(4):
                op_("pe", lambda e, d=d, sl=sl: e.matmul(
                    bank[sl][0:1, :], ones_col[:], xf[d][:, sl * 512:(sl + 1) * 512],
                    start=(d == 0), stop=(d == DC - 1)))
                emitter = (lambda f: op("pe", cPE, f)) if sl == 3 else (lambda f: op_("pe", f))
                pe_x2done[d] = emitter(lambda e, d=d, sl=sl: e.matmul(
                    bank[4 + sl][0:1, :], ones_col[:], x2b[d % 2][:, sl * 512:(sl + 1) * 512],
                    start=(d == 0), stop=(d == DC - 1)))
        v_stats = pe_x2done[DC - 1]

        r_mean = rows_bf[:, 0:2048]
        r_rstd = rows_bf[:, 2048:4096]
        r_var = rows_bf[:, 4096:6144]
        r_msq = rows_bf[:, 6144:8192]
        wait("dve", cPE, v_stats)
        for sl in range(4):
            op("dve", cDVE, lambda e, sl=sl: e.tensor_scalar_mul(
                r_mean[:, sl * 512:(sl + 1) * 512], bank[sl][0:1, :], 1.0 / D))
        wait("dve", cDVE, cDVE.n)
        op("dve", cDVE, lambda e: e.tensor_tensor(r_msq[:], r_mean[:], r_mean[:], ALU.mult))
        wait("dve", cDVE, cDVE.n)
        for sl in range(4):
            op("dve", cDVE, lambda e, sl=sl: e.scalar_tensor_tensor(
                r_var[:, sl * 512:(sl + 1) * 512], bank[4 + sl][0:1, :], 1.0 / D,
                r_msq[:, sl * 512:(sl + 1) * 512], ALU.mult, ALU.subtract))
        v_var = cDVE.n
        wait("act", cDVE, v_var)
        op("act", cACT, lambda e: e.activation(rows_f[:, 0:2048], r_var[:], AF.Ln, bias=cns[:, 0:1]))
        wait("act", cACT, cACT.n)
        v_rstd = op("act", cACT, lambda e: e.activation(r_rstd[:], rows_f[:, 0:2048], AF.Exp, scale=-0.5))
        wait("dve", cACT, v_rstd)
        v_mrs = op("dve", cDVE, lambda e: e.tensor_tensor(r_msq[:], r_mean[:], r_rstd[:], ALU.mult))
        r_mrs = r_msq

        wait("pe", cDVE, v_mrs)
        v_bcmm = 0
        for sl in range(4):
            op_("pe", lambda e, sl=sl: e.matmul(
                bank[sl][:, :], ones_row[:], r_rstd[:, sl * 512:(sl + 1) * 512],
                start=True, stop=True))
            emitter = (lambda f: op("pe", cPE, f)) if sl == 3 else (lambda f: op_("pe", f))
            v_bcmm = emitter(lambda e, sl=sl: e.matmul(
                bank[4 + sl][:, :], ones_row[:], r_mrs[:, sl * 512:(sl + 1) * 512],
                start=True, stop=True))
        wait("dve", cPE, v_bcmm)
        for sl in range(4):
            op("dve", cDVE, lambda e, sl=sl: e.tensor_copy(bc0[:, sl * 512:(sl + 1) * 512], bank[sl][:, :]))
            op("dve", cDVE, lambda e, sl=sl: e.tensor_copy(bc1[:, sl * 512:(sl + 1) * 512], bank[4 + sl][:, :]))
        v_bc = cDVE.n
        for b in range(8):
            bank_rel[b] = (cDVE, v_bc)

        wv = [A(d * 2048, 2048) for d in range(DC)]
        wait("dve", cDVE, v_bc)
        wait("pool", cDVE, v_bc)
        v_xn = [0] * DC
        v_wvl = [0] * DC
        xn_eng = [0] * DC
        last_on = {"dve": 0, "pool": 0}
        for d in range(DC):
            q_ = "pool" if d % 3 == 2 else "dve"
            c_ = cPOOL if d % 3 == 2 else cDVE
            sl_ = 1 if q_ == "pool" else 0
            xn_eng[d] = c_
            if last_on[q_]:
                wait(q_, c_, last_on[q_])
            op(q_, c_, lambda e, d=d, sl_=sl_: e.tensor_tensor(x2b[sl_][:], xf[d][:], bc0[:], ALU.mult))
            wait(q_, c_, c_.n)
            v_xn[d] = op(q_, c_, lambda e, d=d, sl_=sl_: e.tensor_tensor(
                xn[d][:], x2b[sl_][:], bc1[:], ALU.subtract))
            last_on[q_] = v_xn[d]
            if stage != "xn":
                wait("pool", c_, v_xn[d])
                v_wvl[d] = dma("pool", SM["dWV%02d" % d], wv[d], wvT[d * 128:(d + 1) * 128, :])
        v_xn_all_d = last_on["dve"]
        v_xn_all_p = last_on["pool"]

        if stage == "xn":
            wait("pool", cDVE, v_xn_all_d)
            wait("pool", cPOOL, v_xn_all_p)
            for d in range(DC):
                dma("pool", SM["dDBG"], out[d * 128:(d + 1) * 128, :], xn[d][:])
            wait("pool", SM["dDBG"], SM["dDBG"].n)
            return finish(nc, Q)

        dma("pool", SM["dWVb"], bc0[:], bv_row.partition_broadcast(128).squeeze(1))
        v_wvb = SM["dWVb"].n

        vsb = [A(44032 + i * 512, 512) for i in range(2)]
        wait("dve", SM["dWVb"], v_wvb)
        nvt = 0
        for tt in range(16):
            bs = (tt % 2) * 4
            for b in range(bs, bs + 4):
                pe_take(b)
            v_mm = 0
            for i in range(DC):
                if tt == 0:
                    wait("pe", xn_eng[i], v_xn[i])
                    wait("pe", SM["dWV%02d" % i], v_wvl[i])
                for oc in range(4):
                    last = (i == DC - 1 and oc == 3)
                    emitter = (lambda f: op("pe", cPE, f)) if last else (lambda f: op_("pe", f))
                    v_mm = emitter(lambda e, tt=tt, i=i, oc=oc, bs=bs: e.matmul(
                        bank[bs + oc][:, :], xn[i][:, tt * 128:(tt + 1) * 128],
                        wv[i][:, oc * 512:(oc + 1) * 512],
                        start=(i == 0), stop=(i == DC - 1)))
            for oc in range(4):
                slot = nvt % 2
                dsem = SM["dVS%d" % slot]
                wait("dve", dsem, dsem.n)
                wait("dve", cPE, v_mm)
                v_ev = op("dve", cDVE, lambda e, bs=bs, oc=oc, slot=slot: e.tensor_tensor(
                    vsb[slot][:], bank[bs + oc][:, :], bc0[:, oc * 512:(oc + 1) * 512], ALU.add))
                bank_rel[bs + oc] = (cDVE, v_ev)
                wait("sp", cDVE, v_ev)
                dma("sp", dsem, v_scr[tt * 128:(tt + 1) * 128, oc * 512:(oc + 1) * 512], vsb[slot][:])
                nvt += 1
        v_vs0, v_vs1, v_vs2 = SM["dVS0"].n, SM["dVS1"].n, SM["dVS2"].n
        v_vev_all = cDVE.n
        v_vproj_pe = cPE.n

        wqh = [A(0 + i * 2048, 2048) for i in range(3)]
        wkh = [A(6144 + i * 2048, 2048) for i in range(3)]
        vh = [A(12288 + i * 2048, 2048) for i in range(3)]
        Qh = A(18432, 2048)
        Kh = A(20480, 2048)
        q2 = A(22528, 2048)
        atT = [A(24576 + i * 512, 512) for i in range(4)]
        atM = [A(26624 + i * 512, 512) for i in range(4)]
        cst = [A(28672 + i * 512, 512) for i in range(2)]
        dwin = [A(29696 + i * 2048, 2048) for i in range(2)]
        dwout = A(33792, 2048)
        k2b = A(35840, 2048)

        rs_row = rows_f[:, 0:512]
        rs_bf = rows_bf[:, 0:512]

        wait("pool", cDVE, v_xn_all_d)
        dma("pool", SM["dCOL"], bc1[:], ds_row.partition_broadcast(128).squeeze(1))
        v_ds = SM["dCOL"].n

        cache_jobs = []
        for isup in range(NSUP):
            for hb in range(2):
                r0, r1 = hb * 1024, (hb + 1) * 1024
                cache_jobs.append((guc[r0:r1, isup * 1024:isup * 1024 + 512],
                                   gwT[r0:r1, isup * 512:(isup + 1) * 512]))
                cache_jobs.append((guc[r0:r1, isup * 1024 + 512:isup * 1024 + 1024],
                                   uwT[r0:r1, isup * 512:(isup + 1) * 512]))
        for c8 in range(8):
            cache_jobs.append((woc[c8 * 256:(c8 + 1) * 256, :], woT[c8 * 256:(c8 + 1) * 256, :]))
        cache_state = [0]

        def cache_step(n):
            for _ in range(n):
                if cache_state[0] >= len(cache_jobs):
                    return
                o, i = cache_jobs[cache_state[0]]
                dma("pool", SM["dWC"], o, i)
                cache_state[0] += 1

        dwc_state = {"in": 0, "fold": 0}
        v_din = [0] * IC
        v_fold = [0] * IC

        def dwc_issue_in(n):
            for _ in range(n):
                i = dwc_state["in"]
                if i >= IC:
                    return
                slot = i % 2
                if i >= 2:
                    wait("pool", cDVE, v_fold[i - 2])
                v_din[i] = dma("pool", SM["dDI%d" % slot], dwin[slot][:], dwT[i * 128:(i + 1) * 128, :])
                dwc_state["in"] += 1

        def dwc_fold(n):
            for _ in range(n):
                i = dwc_state["fold"]
                if i >= dwc_state["in"] or i >= IC:
                    return
                slot = i % 2
                wait("dve", SM["dDI%d" % slot], v_din[i])
                wait("dve", SM["dCOL"], v_ds)
                if i >= 1:
                    wait("dve", SM["dDO"], i * 16)
                v_fold[i] = op("dve", cDVE, lambda e, slot=slot: e.tensor_tensor(
                    dwout[:], dwin[slot][:], bc1[:], ALU.mult))
                wait("pool", cDVE, v_fold[i])
                dma("pool", SM["dDO"], dwc[i * 128:(i + 1) * 128, :], dwout[:])
                dwc_state["fold"] += 1

        def dwc_step(n):
            for _ in range(n):
                dwc_fold(1)
                dwc_issue_in(1)

        v_wq = [0] * H
        v_wk = [0] * H
        v_vh = [0] * H
        pe_head_done = [0] * H

        def head_weights(h):
            slot = h % 3
            if h >= 3:
                wait("pool", cPE, pe_head_done[h - 3])
            v_wq[h] = dma("pool", SM["dWQ%d" % slot], wqh[slot].rearrange("p (c n) -> p c n", n=128),
                          wqT[:, h * 128:(h + 1) * 128].rearrange("(c p) n -> p c n", p=128))
            v_wk[h] = dma("pool", SM["dWK%d" % slot], wkh[slot].rearrange("p (c n) -> p c n", n=128),
                          wkT[:, h * 128:(h + 1) * 128].rearrange("(c p) n -> p c n", p=128))
            if h == 0:
                wait("pool", SM["dVS0"], v_vs0)
                wait("pool", SM["dVS1"], v_vs1)
                wait("pool", SM["dVS2"], v_vs2)
            v_vh[h] = dma("pool", SM["dVH%d" % slot], vh[slot].rearrange("p (c n) -> p c n", n=128),
                          v_scr[:, h * 128:(h + 1) * 128].rearrange("(c p) n -> p c n", p=128))

        wait("pool", cPE, v_vproj_pe)
        head_weights(0)
        head_weights(1)
        dwc_issue_in(2)

        atT_free = [None, None, None, None]
        atM_free = [None, None, None, None]
        last_ctx = {0: 0, 1: 0}
        v_negc_reader = [0, 0]

        for h in range(H):
            slot = h % 3
            nslot = h % 2
            if h + 2 < H:
                head_weights(h + 2)

            wait("pe", SM["dWQ%d" % slot], v_wq[h])
            wait("act", cDVE, v_bqs)
            v_qk_evac = []
            v_sqq = [0] * 4
            v_sqk = [0] * 4
            v_qn2 = [0]
            v_kn2 = [0]

            def proj_grp(wsb, dst, sqb, v_sq, bias_ap, tc):
                b = 6 + (tc % 2)
                pe_take(b)
                v_mm = 0
                for i in range(DC):
                    emitter = (lambda f: op("pe", cPE, f)) if i == DC - 1 else (lambda f: op_("pe", f))
                    v_mm = emitter(lambda e, wsb=wsb, i=i, tc=tc, b=b: e.matmul(
                        bank[b][:, :], wsb[:, i * 128:(i + 1) * 128],
                        xn[i][:, tc * 512:(tc + 1) * 512],
                        start=(i == 0), stop=(i == DC - 1)))
                wait("act", cPE, v_mm)
                v_ev = op("act", cACT, lambda e, dst=dst, tc=tc, b=b, bias_ap=bias_ap, h=h: e.activation(
                    dst[:, tc * 512:(tc + 1) * 512], bank[b][:, :], AF.Identity,
                    bias=bias_ap[:, h:h + 1], scale=SCALE))
                bank_rel[b] = (cACT, v_ev)
                v_qk_evac.append(v_ev)
                wait("act", cACT, v_ev)
                v_sq[tc] = op("act", cACT, lambda e, dst=dst, sqb=sqb, tc=tc: e.activation(
                    sqb[:, tc * 512:(tc + 1) * 512], dst[:, tc * 512:(tc + 1) * 512], AF.Square))

            def sum_mm(sqb, v_sq, bsum, sl, vacc):
                if sl == 0:
                    pe_take(bsum)
                wait("pe", cACT, v_sq[sl])
                emitter = (lambda f: op("pe", cPE, f)) if sl == 3 else (lambda f: op_("pe", f))
                vacc[0] = emitter(lambda e, sqb=sqb, bsum=bsum, sl=sl: e.matmul(
                    bank[bsum][0:1, :], ones_col[:], sqb[:, sl * 512:(sl + 1) * 512],
                    start=(sl == 0), stop=(sl == 3)))

            for tc in range(4):
                proj_grp(wqh[slot], Qh, q2, v_sqq, bqs, tc)
                if tc >= 1:
                    sum_mm(q2, v_sqq, 0, tc - 1, v_qn2)
            dwc_step(2)
            cache_step(3)
            wait("pe", SM["dWK%d" % slot], v_wk[h])
            for tc in range(4):
                proj_grp(wkh[slot], Kh, k2b, v_sqk, bks, tc)
                if tc == 0:
                    sum_mm(q2, v_sqq, 0, 3, v_qn2)
                else:
                    sum_mm(k2b, v_sqk, 1, tc - 1, v_kn2)
            wait("dve", cPE, v_qn2[0])
            v_qm = op("dve", cDVE, lambda e: e.reduce_max(ktmp[:, 4:5], bank[0][0:1, :], axis=AX.X))
            bank_rel[0] = (cDVE, v_qm)

            pre_sc = {}
            for j in range(3):
                b_s = [4, 5, 6][j % 3]
                pe_take(b_s)
                pre_sc[(0, j)] = op("pe", cPE, lambda e, b_s=b_s, j=j: e.matmul(
                    bank[b_s][:, :], Kh[:, j * 128:(j + 1) * 128],
                    Qh[:, 0:512], start=True, stop=True))
            sum_mm(k2b, v_sqk, 1, 3, v_kn2)
            wait("dve", cPE, v_kn2[0])
            v_km = op("dve", cDVE, lambda e: e.reduce_max(ktmp[:, 12:13], bank[1][0:1, :], axis=AX.X))
            bank_rel[1] = (cDVE, v_km)
            wait("dve", cDVE, v_km)
            v_negc_row = op("dve", cDVE, lambda e: e.tensor_tensor(ktmp[:, 6:7], ktmp[:, 4:5], ktmp[:, 12:13], ALU.add))
            pe_take(7)
            wait("pe", cDVE, v_negc_row)
            v_ncmm = op("pe", cPE, lambda e: e.matmul(
                bank[7][:, 0:1], halfneg_row[:], ktmp[:, 6:7], start=True, stop=True))
            wait("dve", cPE, v_ncmm)
            wait("dve", cACT, v_negc_reader[nslot])
            v_negc = op("dve", cDVE, lambda e, nslot=nslot: e.tensor_copy(negc[nslot][:], bank[7][:, 0:1]))
            bank_rel[7] = (cDVE, v_negc)
            wait("act", cDVE, v_negc)

            wait("pe", SM["dVH%d" % slot], v_vh[h])
            wait("dve", cPOOL, v_mask)
            for qs in range(4):
                njs = 4 * qs + 4
                b_ctx = [0, 3][qs % 2]
                b_sum = [1, 2][qs % 2]
                v_exp = {}
                v_mskd = {}
                v_avsum = {}
                src_of = {}

                def emit_scores(j, qs=qs, njs=njs):
                    jp = j % 4
                    b_s = [4, 5, 6][j % 3]
                    if (qs, j) in pre_sc:
                        v_sc = pre_sc[(qs, j)]
                    else:
                        pe_take(b_s)
                        v_sc = op("pe", cPE, lambda e, b_s=b_s, j=j, qs=qs: e.matmul(
                            bank[b_s][:, :], Kh[:, j * 128:(j + 1) * 128],
                            Qh[:, qs * 512:(qs + 1) * 512], start=True, stop=True))
                    wait("act", cPE, v_sc)
                    if atT_free[jp] is not None:
                        sem, v = atT_free[jp]
                        wait("act", sem, v)
                    v_exp[j] = op("act", cACT, lambda e, jp=jp, b_s=b_s, nslot=nslot: e.activation(
                        atT[jp][:], bank[b_s][:, :], AF.Exp, bias=negc[nslot][:]))
                    v_negc_reader[nslot] = v_exp[j]
                    bank_rel[b_s] = (cACT, v_exp[j])
                    m = j - 4 * qs
                    if m >= 0:
                        wait("dve", cACT, v_exp[j])
                        if atM_free[jp] is not None:
                            sem, v = atM_free[jp]
                            wait("dve", sem, v)
                        v_mskd[j] = op("dve", cDVE, lambda e, jp=jp, m=m: e.tensor_tensor(
                            atM[jp][:], atT[jp][:], mask[:, 384 - 128 * m: 896 - 128 * m], ALU.mult))
                        src_of[j] = atM[jp]
                    else:
                        src_of[j] = atT[jp]

                def emit_av(j, qs=qs, njs=njs, b_ctx=b_ctx, b_sum=b_sum):
                    jp = j % 4
                    src = src_of[j]
                    if j == 0:
                        pe_take(b_ctx)
                        pe_take(b_sum)
                    if j in v_mskd:
                        wait("pe", cDVE, v_mskd[j])
                    else:
                        wait("pe", cACT, v_exp[j])
                    op_("pe", lambda e, j=j, src=src, slot=slot, njs=njs, b_ctx=b_ctx: e.matmul(
                        bank[b_ctx][:, :], vh[slot][:, j * 128:(j + 1) * 128], src[:],
                        start=(j == 0), stop=(j == njs - 1)))
                    v_avsum[j] = op("pe", cPE, lambda e, j=j, src=src, njs=njs, b_sum=b_sum: e.matmul(
                        bank[b_sum][:, :], ones128[:], src[:],
                        start=(j == 0), stop=(j == njs - 1)))
                    atT_free[jp] = (cPE, v_avsum[j])
                    if j in v_mskd:
                        atM_free[jp] = (cPE, v_avsum[j])

                for j in range(njs):
                    emit_scores(j)
                    if j >= 3:
                        emit_av(j - 3)
                for j in range(max(0, njs - 3), njs):
                    emit_av(j)
                v_stripe_mm = v_avsum[njs - 1]

                sl2 = qs % 2
                wait("act", cPE, v_stripe_mm)
                op("act", cACT, lambda e, sl2=sl2, b_sum=b_sum: e.activation(
                    rs_sb[sl2][:], bank[b_sum][:, :], AF.Ln))
                wait("act", cACT, cACT.n)
                if last_ctx[1 - sl2]:
                    wait("act", cDVE, last_ctx[1 - sl2])
                v_cp = op("act", cACT, lambda e, sl2=sl2: e.activation(
                    rs_sb[1 - sl2][:], rs_sb[sl2][:], AF.Exp, scale=-1.0))
                bank_rel[b_sum] = (cACT, v_cp)
                wait("dve", cACT, v_cp)
                csl = (h * 4 + qs) % 2
                dsem = SM["dCX%d" % csl]
                wait("dve", dsem, dsem.n)
                v_ctx = op("dve", cDVE, lambda e, sl2=sl2, csl=csl, b_ctx=b_ctx: e.tensor_tensor(
                    cst[csl][:], bank[b_ctx][:, :], rs_sb[1 - sl2][:], ALU.mult))
                last_ctx[1 - sl2] = v_ctx
                bank_rel[b_ctx] = (cDVE, v_ctx)
                wait("sp", cDVE, v_ctx)
                dma("sp", dsem, ctx_scr[h * 128:(h + 1) * 128, qs * 512:(qs + 1) * 512], cst[csl][:])
            pe_head_done[h] = cPE.n
            dwc_step(1)
            cache_step(1)

        while dwc_state["fold"] < IC:
            dwc_issue_in(1)
            dwc_fold(1)
        cache_step(len(cache_jobs))
        v_dwc = SM["dDO"].n
        v_wc = SM["dWC"].n
        v_ctx_all0, v_ctx_all1 = SM["dCX0"].n, SM["dCX1"].n

        sg = [A(i * 512, 512) for i in range(IC)]
        csl_t = A(22528, 8192)
        gut = [A(30720 + i * 1024, 1024) for i in range(6)]
        dk = [A(36864 + i * 1024, 1024) for i in range(3)]
        wkt = [A(39936 + i * 1024, 1024) for i in range(3)]
        sgt = [A(43008 + i * 512, 512) for i in range(2)]

        wait("pool", cDVE, v_vev_all)
        dma("pool", SM["dCOL"], bc0[:], bo_row.partition_broadcast(128).squeeze(1))
        v_bo = SM["dCOL"].n

        full = (stage == "full")
        v_gumm = {}
        ngt = [0]
        v_dkmm = {}
        ndk = [0]
        v_womm = {}
        nwo = [0]
        v_stt = {}
        v_ep1 = {}
        v_ep2_slot = {0: 0, 1: 0}
        nep = [0]
        pe_tc_done = 0

        wait("dve", SM["dCX0"], v_ctx_all0)
        wait("dve", SM["dCX1"], v_ctx_all1)
        wait("sp", SM["dWC"], v_wc)
        wait("sp", SM["dDO"], v_dwc)
        wait("act", SM["dWC"], v_wc)
        wait("act", SM["dDO"], v_dwc)
        for tc in range(4):
            wait("pool", SM["dCX0"], v_ctx_all0)
            wait("pool", SM["dCX1"], v_ctx_all1)
            wait("pool", SM["dDO"], v_dwc)
            wait("pool", SM["dWC"], v_wc)
            if tc >= 1:
                wait("pool", cPE, pe_tc_done)
            dma("pool", SM["dCS"], csl_t.rearrange("p (c n) -> p c n", n=512),
                ctx_scr[:, tc * 512:(tc + 1) * 512].rearrange("(c p) n -> p c n", p=128))
            v_cs = SM["dCS"].n

            if full:
                for isup in range(NSUP):
                    for hh in range(DC):
                        k = ngt[0]
                        gslot = k % 6
                        gsem = SM["dGT%d" % gslot]
                        if k >= 6:
                            wait("sp", cPE, v_gumm[k - 6])
                        v_g = dma("sp", gsem, gut[gslot][:],
                                  guc[hh * 128:(hh + 1) * 128, isup * 1024:(isup + 1) * 1024])
                        wait("pe", gsem, v_g)
                        last_hh = (hh == DC - 1)
                        v_gb = [0] * 8
                        for icc in range(4):
                            if hh == 0:
                                pe_take(icc)
                            emitter = (lambda f: op("pe", cPE, f)) if last_hh else (lambda f: op_("pe", f))
                            v_gb[icc] = emitter(lambda e, gslot=gslot, icc=icc, hh=hh, tc=tc: e.matmul(
                                bank[icc][:, :], gut[gslot][:, icc * 128:(icc + 1) * 128],
                                xn[hh][:, tc * 512:(tc + 1) * 512],
                                start=(hh == 0), stop=(hh == DC - 1)))
                        for icc in range(4):
                            if hh == 0:
                                pe_take(4 + icc)
                            emitter = (lambda f: op("pe", cPE, f)) if (last_hh or icc == 3) else (lambda f: op_("pe", f))
                            v_gb[4 + icc] = emitter(lambda e, gslot=gslot, icc=icc, hh=hh, tc=tc: e.matmul(
                                bank[4 + icc][:, :], gut[gslot][:, 512 + icc * 128: 512 + (icc + 1) * 128],
                                xn[hh][:, tc * 512:(tc + 1) * 512],
                                start=(hh == 0), stop=(hh == DC - 1)))
                        v_gumm[k] = v_gb[7] if last_hh else v_gb[7]
                        ngt[0] += 1
                    for icc in range(4):
                        idx = isup * 4 + icc
                        wait("act", cPE, v_gb[icc])
                        if idx >= 2:
                            wait("act", cDVE, v_stt.get((tc, idx - 2), 0))
                        v_si = op("act", cACT, lambda e, icc=icc, idx=idx: e.activation(
                            sgt[idx % 2][:], bank[icc][:, :], AF.Silu, scale=gs_s[:, idx:idx + 1]))
                        bank_rel[icc] = (cACT, v_si)
                        wait("dve", cACT, v_si)
                        wait("dve", cPE, v_gb[4 + icc])
                        v_stt[(tc, idx)] = op("dve", cDVE, lambda e, icc=icc, idx=idx: e.scalar_tensor_tensor(
                            sg[idx][:], bank[4 + icc][:, :], us_s[:, idx:idx + 1],
                            sgt[idx % 2][:], ALU.mult, ALU.mult))
                        bank_rel[4 + icc] = (cDVE, v_stt[(tc, idx)])
                v_gu_done = v_stt[(tc, IC - 1)]

            for hhalf in range(2):
                if full:
                    wait("pe", cDVE, v_gu_done)
                    for i in range(IC):
                        k = ndk[0]
                        dslot = k % 3
                        dsem = SM["dDK%d" % dslot]
                        if k >= 3:
                            wait("sp", cPE, v_dkmm[k - 3])
                        v_d = dma("sp", dsem, dk[dslot][:],
                                  dwc[i * 128:(i + 1) * 128, hhalf * 1024:(hhalf + 1) * 1024])
                        wait("pe", dsem, v_d)
                        for ts in range(4):
                            for hc in range(2):
                                if i == 0:
                                    pe_take(ts * 2 + hc)
                                last = (ts == 3 and hc == 1)
                                emitter = (lambda f: op("pe", cPE, f)) if last else (lambda f: op_("pe", f))
                                v_dkmm[k] = emitter(lambda e, i=i, ts=ts, hc=hc, dslot=dslot: e.matmul(
                                    bank[ts * 2 + hc][:, :], sg[i][:, ts * 128:(ts + 1) * 128],
                                    dk[dslot][:, hc * 512:(hc + 1) * 512],
                                    start=(i == 0), stop=False))
                        ndk[0] += 1
                wait("pe", SM["dCS"], v_cs)
                v_ob = [0] * 8
                for dd in range(DC):
                    k = nwo[0]
                    wslot = k % 3
                    wsem = SM["dWO%d" % wslot]
                    if k >= 3:
                        wait("act", cPE, v_womm[k - 3])
                    v_w = dma("act", wsem, wkt[wslot][:],
                              woc[dd * 128:(dd + 1) * 128, hhalf * 1024:(hhalf + 1) * 1024])
                    wait("pe", wsem, v_w)
                    last_dd = (dd == DC - 1)
                    for ts in range(4):
                        for hc in range(2):
                            if not full and dd == 0:
                                pe_take(ts * 2 + hc)
                            last = (ts == 3 and hc == 1)
                            emitter = (lambda f: op("pe", cPE, f)) if (last or last_dd) else (lambda f: op_("pe", f))
                            v_ob[ts * 2 + hc] = emitter(lambda e, dd=dd, ts=ts, hc=hc, wslot=wslot: e.matmul(
                                bank[ts * 2 + hc][:, :],
                                csl_t[:, dd * 512 + ts * 128: dd * 512 + (ts + 1) * 128],
                                wkt[wslot][:, hc * 512:(hc + 1) * 512],
                                start=(not full and dd == 0), stop=(dd == DC - 1)))
                    v_womm[k] = v_ob[7]
                    nwo[0] += 1
                v_all_mm = v_womm[nwo[0] - 1]
                pe_tc_done = v_all_mm

                v_xld = {}
                ep_base = nep[0]

                def emit_xload(bidx, tc=tc, hhalf=hhalf, ep_base=ep_base):
                    ts, hc = bidx // 2, bidx % 2
                    t0 = tc * 512 + ts * 128
                    h0 = hhalf * 1024 + hc * 512
                    k = ep_base + bidx
                    xslot = k % 4
                    xsem = SM["dXT%d" % xslot]
                    if k >= 4 and (k - 4) in v_ep1:
                        wait("act", cDVE, v_ep1[k - 4])
                    v_xld[bidx] = (xsem, dma("act", xsem, x_t[xslot][:],
                                             x_tm[t0:t0 + 128, h0:h0 + 512]), xslot)

                for bidx in range(4):
                    emit_xload(bidx)

                for ts in range(4):
                    for hc in range(2):
                        bidx = ts * 2 + hc
                        t0 = tc * 512 + ts * 128
                        h0 = hhalf * 1024 + hc * 512
                        k = nep[0]
                        xsem, v_x, xslot = v_xld[bidx]
                        wait("dve", xsem, v_x)
                        wait("dve", cPE, v_ob[bidx])
                        wait("dve", SM["dCOL"], v_bo)
                        if v_ep2_slot[k % 2]:
                            wait("dve", cPOOL, v_ep2_slot[k % 2])
                        v_ep1[k] = op("dve", cDVE, lambda e, bidx=bidx, xslot=xslot, k=k: e.tensor_tensor(
                            s1_t[k % 2][:], bank[bidx][:, :], x_t[xslot][:], ALU.add))
                        bank_rel[bidx] = (cDVE, v_ep1[k])
                        osem = SM["dOT%d" % (k % 2)]
                        wait("pool", cDVE, v_ep1[k])
                        wait("pool", osem, osem.n)
                        v_ep2 = op("pool", cPOOL, lambda e, k=k, h0=h0: e.tensor_tensor(
                            out_t[k % 2][:], s1_t[k % 2][:], bc0[:, h0:h0 + 512], ALU.add))
                        wait("pool", cPOOL, v_ep2)
                        dma("pool", osem, out[t0:t0 + 128, h0:h0 + 512], out_t[k % 2][:])
                        v_ep2_slot[k % 2] = v_ep2
                        if bidx + 4 < 8:
                            emit_xload(bidx + 4)
                        nep[0] += 1

        wait("pool", SM["dOT0"], SM["dOT0"].n)
        wait("pool", SM["dOT1"], SM["dOT1"].n)
        return finish(nc, Q)


def finish(nc, Q):
    with nc.Block() as block:
        @block.tensor
        def _(e):
            for f in Q["pe"]:
                f(e)

        @block.scalar
        def _(e):
            for f in Q["act"]:
                f(e)

        @block.vector
        def _(e):
            for f in Q["dve"]:
                f(e)

        @block.gpsimd
        def _(e):
            for f in Q["pool"]:
                f(e)

        @block.sync
        def _(e):
            for f in Q["sp"]:
                f(e)
    return nc


def _prep_inputs(inputs):
    c = np.ascontiguousarray
    shared = {
        "wqT": c(inputs["L0_attn_Wq"].T), "wkT": c(inputs["L0_attn_Wk"].T),
        "wvT": c(inputs["L0_attn_Wv"].T), "woT": c(inputs["L0_attn_Wo"].T),
        "gwT": c(inputs["L0_ffn_gate_w"].T), "uwT": c(inputs["L0_ffn_up_w"].T),
        "dwT": c(inputs["L0_ffn_down_w"].T),
        "bq_col": c(np.asarray(inputs["L0_attn_bq"]).reshape(H, 128).T),
        "bk_col": c(np.asarray(inputs["L0_attn_bk"]).reshape(H, 128).T),
        "bv_row": c(np.asarray(inputs["L0_attn_bv"]).reshape(1, D)),
        "bo_row": c(np.asarray(inputs["L0_attn_bo"]).reshape(1, D)),
        "ds_row": c(np.asarray(inputs["L0_ffn_down_s"]).reshape(1, D)),
        "gs_col": c(np.asarray(inputs["L0_ffn_gate_s"]).reshape(IC, 128).T),
        "us_col": c(np.asarray(inputs["L0_ffn_up_s"]).reshape(IC, 128).T),
    }
    shared = {k: np.asarray(v, dtype=np.float32) for k, v in shared.items()}
    x = np.asarray(inputs["L0_attn_X"], dtype=np.float32)
    in_maps = []
    for b in range(8):
        xb = c(x[b])
        m = dict(shared)
        m["x_tm"] = xb
        m["x_fm"] = c(xb.T)
        in_maps.append(m)
    return in_maps


def kernel(**inputs):
    in_maps = _prep_inputs(inputs)
    nc = build_nc(STAGE)
    res = run_bass_kernel_spmd(nc, in_maps, core_ids=list(range(8)))
    return np.stack([np.asarray(r["out"]) for r in res.results], axis=0)


# version 42
# speedup vs baseline: 1.0238x; 1.0238x over previous
import numpy as np
from contextlib import ExitStack

import concourse.bass as bass
import concourse.mybir as mybir
from concourse.bass_utils import run_bass_kernel_spmd

F32 = mybir.dt.float32
BF16 = mybir.dt.bfloat16
AF = mybir.ActivationFunctionType
ALU = mybir.AluOpType
AX = mybir.AxisListType

S = 2048
D = 2048
H = 16
DH = 128
NI = 5632
DC = 16
IC = 44
NSUP = 11
SCALE = 128.0 ** -0.25
EPS = 1e-5

STAGE = "full"


class Sem:
    def __init__(self, h):
        self.h = h
        self.n = 0

    def inc(self, k=1):
        self.n += k
        return self.n


def build_nc(stage=STAGE):
    nc = bass.Bass("TRN2", target_bir_lowering=False)

    def din(name, shape, dtype=F32):
        return nc.dram_tensor(name, shape, dtype, kind="ExternalInput").ap()

    x_tm = din("x_tm", [S, D])
    x_fm = din("x_fm", [D, S])
    wqT = din("wqT", [D, D])
    wkT = din("wkT", [D, D])
    wvT = din("wvT", [D, D])
    woT = din("woT", [D, D])
    gwT = din("gwT", [D, NI])
    uwT = din("uwT", [D, NI])
    dwT = din("dwT", [NI, D])
    bq_col = din("bq_col", [128, H])
    bk_col = din("bk_col", [128, H])
    bv_row = din("bv_row", [1, D])
    bo_row = din("bo_row", [1, D])
    ds_row = din("ds_row", [1, D])
    gs_col = din("gs_col", [128, IC])
    us_col = din("us_col", [128, IC])

    out = nc.dram_tensor("out", [S, D], F32, kind="ExternalOutput").ap()
    v_scr = nc.dram_tensor("v_scr", [S, D], BF16, kind="Internal").ap()
    ctx_scr = nc.dram_tensor("ctx_scr", [D, S], BF16, kind="Internal").ap()
    dwc = nc.dram_tensor("dwc", [NI, D], BF16, kind="Internal").ap()
    guc = nc.dram_tensor("guc", [D, 2 * NI], BF16, kind="Internal").ap()
    woc = nc.dram_tensor("woc", [D, D], BF16, kind="Internal").ap()

    REG_W = 45056

    with ExitStack() as st:
        ec = st.enter_context
        xn_t = ec(nc.sbuf_tensor("xn", [128, DC * S], BF16))
        rega = ec(nc.sbuf_tensor("rega", [128, REG_W], BF16))
        bc0 = ec(nc.sbuf_tensor("bc0", [128, D], BF16))
        bc1 = ec(nc.sbuf_tensor("bc1", [128, D], BF16))
        rows_bf = ec(nc.sbuf_tensor("rows_bf", [1, 8192], BF16))
        rows_f = ec(nc.sbuf_tensor("rows_f", [1, 2048], F32))
        ktmp = ec(nc.sbuf_tensor("ktmp", [1, 16], F32))
        negc = [ec(nc.sbuf_tensor(f"negc{i}", [128, 1], F32)) for i in range(2)]
        rs_sb = [ec(nc.sbuf_tensor(f"rs_sb{i}", [128, 512], F32)) for i in range(2)]
        x_t = [ec(nc.sbuf_tensor(f"x_t{i}", [128, 512], F32)) for i in range(4)]
        s1_t = [ec(nc.sbuf_tensor(f"s1_t{i}", [128, 512], F32)) for i in range(2)]
        out_t = [ec(nc.sbuf_tensor(f"out_t{i}", [128, 512], F32)) for i in range(2)]
        mask = ec(nc.sbuf_tensor("mask", [128, 896], BF16))
        ones_col = ec(nc.sbuf_tensor("ones_col", [128, 1], BF16))
        ones128 = ec(nc.sbuf_tensor("ones128", [128, 128], BF16))
        ones_row = ec(nc.sbuf_tensor("ones_row", [1, 128], BF16))
        halfneg_row = ec(nc.sbuf_tensor("halfneg_row", [1, 128], F32))
        bq_s = ec(nc.sbuf_tensor("bq_s", [128, H], F32))
        bk_s = ec(nc.sbuf_tensor("bk_s", [128, H], F32))
        bqs = ec(nc.sbuf_tensor("bqs", [128, H], F32))
        bks = ec(nc.sbuf_tensor("bks", [128, H], F32))
        gs_s = ec(nc.sbuf_tensor("gs_s", [128, IC], F32))
        us_s = ec(nc.sbuf_tensor("us_s", [128, IC], F32))
        cns = ec(nc.sbuf_tensor("cns", [1, 4], F32))
        bank = [ec(nc.psum_tensor(f"bank{i}", [128, 512], F32)) for i in range(8)]

        semnames = ["cPE", "cACT", "cDVE", "cPOOL",
                    "dXF", "dWVb", "dCOL","dWV00", "dWV01", "dWV02", "dWV03", "dWV04", "dWV05", "dWV06", "dWV07", "dWV08", "dWV09", "dWV10", "dWV11", "dWV12", "dWV13", "dWV14", "dWV15", 
                    "dWQ0", "dWQ1", "dWQ2", "dWK0", "dWK1", "dWK2", "dVH0", "dVH1", "dVH2",
                    "dGT0", "dGT1", "dGT2", "dGT3", "dGT4", "dGT5",
                    "dUT0", "dUT1", "dUT2", "dUT3",
                    "dDK0", "dDK1", "dDK2", "dDK3",
                    "dWO0", "dWO1", "dWO2", "dWO3", "dCS", "dDI0", "dDI1", "dDO", "dWC",
                    "dXT0", "dXT1", "dXT2", "dXT3", "dOT0", "dOT1", "dVS0", "dVS1", "dVS2",
                    "dCX0", "dCX1", "dDBG"]
        SM = {n: Sem(ec(nc.semaphore(n))) for n in semnames}
        cPE, cACT, cDVE, cPOOL = SM["cPE"], SM["cACT"], SM["cDVE"], SM["cPOOL"]

        Q = {"pe": [], "act": [], "dve": [], "pool": [], "sp": []}
        last_wait = {}

        def em(q, fn):
            Q[q].append(fn)

        def wait(q, sem, val):
            if val <= 0:
                return
            key = (q, sem.h)
            if last_wait.get(key, 0) >= val:
                return
            last_wait[key] = val
            em(q, lambda e, s=sem.h, v=val: e.wait_ge(s, v))

        def dma(q, sem, out_ap, in_ap):
            v = sem.inc(16)
            em(q, lambda e, o=out_ap, i=in_ap, s=sem.h: e.dma_start(out=o, in_=i).then_inc(s, 16))
            return v

        def op(q, sem, fn):
            v = sem.inc(1)
            em(q, lambda e, f=fn, s=sem.h: f(e).then_inc(s, 1))
            return v

        def op_(q, fn):
            em(q, fn)

        bank_rel = [None] * 8

        def pe_take(b):
            if bank_rel[b] is not None:
                sem, v = bank_rel[b]
                wait("pe", sem, v)
                bank_rel[b] = None

        def A(off, n):
            return rega[:, off:off + n]

        op("dve", cDVE, lambda e: e.memset(ones_col[:], 1.0))
        op("dve", cDVE, lambda e: e.memset(ones128[:], 1.0))
        op("dve", cDVE, lambda e: e.memset(ones_row[:], 1.0))
        op("dve", cDVE, lambda e: e.memset(halfneg_row[:], -0.5))
        op("dve", cDVE, lambda e: e.memset(cns[:, 0:1], EPS))
        v_ones = op("dve", cDVE, lambda e: e.memset(cns[:, 1:2], 1e-20))
        wait("pool", cDVE, v_ones)
        v_mask = op("pool", cPOOL, lambda e: e.affine_select(
            mask[:], ones_col[:].broadcast_to([128, 896]), [[1, 896]], ALU.is_ge, 0.0,
            base=-384, channel_multiplier=-1))

        for (dst, src) in [(bq_s, bq_col), (bk_s, bk_col), (gs_s, gs_col), (us_s, us_col)]:
            dma("pool", SM["dCOL"], dst[:], src)
        v_col = SM["dCOL"].n
        wait("dve", SM["dCOL"], v_col)
        op("dve", cDVE, lambda e: e.tensor_scalar_mul(bqs[:], bq_s[:], SCALE))
        v_bqs = op("dve", cDVE, lambda e: e.tensor_scalar_mul(bks[:], bk_s[:], SCALE))

        xf = [A(d * 2048, 2048) for d in range(DC)]
        for d in range(DC):
            dma("pool", SM["dXF"], xf[d], x_fm[d * 128:(d + 1) * 128, :])
        v_xf = SM["dXF"].n

        xn = [xn_t[:, d * S:(d + 1) * S] for d in range(DC)]
        x2b = [A(32768 + i * 2048, 2048) for i in range(2)]

        wait("dve", SM["dXF"], v_xf)
        wait("pe", cDVE, v_ones)
        pe_x2done = [0] * DC
        for d in range(DC):
            if d >= 2:
                wait("dve", cPE, pe_x2done[d - 2])
            v_x2 = op("dve", cDVE, lambda e, d=d: e.tensor_tensor(
                x2b[d % 2][:], xf[d][:], xf[d][:], ALU.mult))
            wait("pe", cDVE, v_x2)
            if d == 0:
                for b in range(8):
                    pe_take(b)
            for sl in range(4):
                op_("pe", lambda e, d=d, sl=sl: e.matmul(
                    bank[sl][0:1, :], ones_col[:], xf[d][:, sl * 512:(sl + 1) * 512],
                    start=(d == 0), stop=(d == DC - 1)))
                emitter = (lambda f: op("pe", cPE, f)) if sl == 3 else (lambda f: op_("pe", f))
                pe_x2done[d] = emitter(lambda e, d=d, sl=sl: e.matmul(
                    bank[4 + sl][0:1, :], ones_col[:], x2b[d % 2][:, sl * 512:(sl + 1) * 512],
                    start=(d == 0), stop=(d == DC - 1)))
        v_stats = pe_x2done[DC - 1]

        r_mean = rows_bf[:, 0:2048]
        r_rstd = rows_bf[:, 2048:4096]
        r_var = rows_bf[:, 4096:6144]
        r_msq = rows_bf[:, 6144:8192]
        wait("dve", cPE, v_stats)
        for sl in range(4):
            op("dve", cDVE, lambda e, sl=sl: e.tensor_scalar_mul(
                r_mean[:, sl * 512:(sl + 1) * 512], bank[sl][0:1, :], 1.0 / D))
        wait("dve", cDVE, cDVE.n)
        op("dve", cDVE, lambda e: e.tensor_tensor(r_msq[:], r_mean[:], r_mean[:], ALU.mult))
        wait("dve", cDVE, cDVE.n)
        for sl in range(4):
            op("dve", cDVE, lambda e, sl=sl: e.scalar_tensor_tensor(
                r_var[:, sl * 512:(sl + 1) * 512], bank[4 + sl][0:1, :], 1.0 / D,
                r_msq[:, sl * 512:(sl + 1) * 512], ALU.mult, ALU.subtract))
        v_var = cDVE.n
        wait("act", cDVE, v_var)
        op("act", cACT, lambda e: e.activation(rows_f[:, 0:2048], r_var[:], AF.Ln, bias=cns[:, 0:1]))
        wait("act", cACT, cACT.n)
        v_rstd = op("act", cACT, lambda e: e.activation(r_rstd[:], rows_f[:, 0:2048], AF.Exp, scale=-0.5))
        wait("dve", cACT, v_rstd)
        v_mrs = op("dve", cDVE, lambda e: e.tensor_tensor(r_msq[:], r_mean[:], r_rstd[:], ALU.mult))
        r_mrs = r_msq

        wait("pe", cDVE, v_mrs)
        v_bcmm = 0
        for sl in range(4):
            op_("pe", lambda e, sl=sl: e.matmul(
                bank[sl][:, :], ones_row[:], r_rstd[:, sl * 512:(sl + 1) * 512],
                start=True, stop=True))
            emitter = (lambda f: op("pe", cPE, f)) if sl == 3 else (lambda f: op_("pe", f))
            v_bcmm = emitter(lambda e, sl=sl: e.matmul(
                bank[4 + sl][:, :], ones_row[:], r_mrs[:, sl * 512:(sl + 1) * 512],
                start=True, stop=True))
        wait("dve", cPE, v_bcmm)
        for sl in range(4):
            op("dve", cDVE, lambda e, sl=sl: e.tensor_copy(bc0[:, sl * 512:(sl + 1) * 512], bank[sl][:, :]))
            op("dve", cDVE, lambda e, sl=sl: e.tensor_copy(bc1[:, sl * 512:(sl + 1) * 512], bank[4 + sl][:, :]))
        v_bc = cDVE.n
        for b in range(8):
            bank_rel[b] = (cDVE, v_bc)

        wv = [A(d * 2048, 2048) for d in range(DC)]
        wait("dve", cDVE, v_bc)
        wait("pool", cDVE, v_bc)
        v_xn = [0] * DC
        xn_eng = [0] * DC
        last_on = {"dve": 0, "pool": 0}
        for d in range(DC):
            q_ = "pool" if d % 3 == 2 else "dve"
            c_ = cPOOL if d % 3 == 2 else cDVE
            sl_ = 1 if q_ == "pool" else 0
            xn_eng[d] = c_
            if last_on[q_]:
                wait(q_, c_, last_on[q_])
            op(q_, c_, lambda e, d=d, sl_=sl_: e.tensor_tensor(x2b[sl_][:], xf[d][:], bc0[:], ALU.mult))
            wait(q_, c_, c_.n)
            v_xn[d] = op(q_, c_, lambda e, d=d, sl_=sl_: e.tensor_tensor(
                xn[d][:], x2b[sl_][:], bc1[:], ALU.subtract))
            last_on[q_] = v_xn[d]
            if stage != "xn":
                wait("pool", c_, v_xn[d])
                dma("pool", SM["dWV%02d" % d], wv[d], wvT[d * 128:(d + 1) * 128, :])
        v_xn_all_d = last_on["dve"]
        v_xn_all_p = last_on["pool"]

        if stage == "xn":
            wait("pool", cDVE, v_xn_all_d)
            wait("pool", cPOOL, v_xn_all_p)
            for d in range(DC):
                dma("pool", SM["dDBG"], out[d * 128:(d + 1) * 128, :], xn[d][:])
            wait("pool", SM["dDBG"], SM["dDBG"].n)
            return finish(nc, Q)

        dma("pool", SM["dWVb"], bc0[:], bv_row.partition_broadcast(128).squeeze(1))
        v_wvb = SM["dWVb"].n

        vsb = [A(44032 + i * 512, 512) for i in range(2)]
        wait("dve", SM["dWVb"], v_wvb)
        nvt = 0
        for tt in range(16):
            bs = (tt % 2) * 4
            for b in range(bs, bs + 4):
                pe_take(b)
            v_mm = 0
            for i in range(DC):
                if tt == 0:
                    wait("pe", xn_eng[i], v_xn[i])
                    wait("pe", SM["dWV%02d" % i], 16)
                for oc in range(4):
                    last = (i == DC - 1 and oc == 3)
                    emitter = (lambda f: op("pe", cPE, f)) if last else (lambda f: op_("pe", f))
                    v_mm = emitter(lambda e, tt=tt, i=i, oc=oc, bs=bs: e.matmul(
                        bank[bs + oc][:, :], xn[i][:, tt * 128:(tt + 1) * 128],
                        wv[i][:, oc * 512:(oc + 1) * 512],
                        start=(i == 0), stop=(i == DC - 1)))
            for oc in range(4):
                slot = nvt % 2
                dsem = SM["dVS%d" % slot]
                wait("dve", dsem, dsem.n)
                wait("dve", cPE, v_mm)
                v_ev = op("dve", cDVE, lambda e, bs=bs, oc=oc, slot=slot: e.tensor_tensor(
                    vsb[slot][:], bank[bs + oc][:, :], bc0[:, oc * 512:(oc + 1) * 512], ALU.add))
                bank_rel[bs + oc] = (cDVE, v_ev)
                wait("sp", cDVE, v_ev)
                dma("sp", dsem, v_scr[tt * 128:(tt + 1) * 128, oc * 512:(oc + 1) * 512], vsb[slot][:])
                nvt += 1
        v_vs0, v_vs1, v_vs2 = SM["dVS0"].n, SM["dVS1"].n, SM["dVS2"].n
        v_vev_all = cDVE.n
        v_vproj_pe = cPE.n

        wqh = [A(0 + i * 2048, 2048) for i in range(3)]
        wkh = [A(6144 + i * 2048, 2048) for i in range(3)]
        vh = [A(12288 + i * 2048, 2048) for i in range(3)]
        Qh = A(18432, 2048)
        Kh = A(20480, 2048)
        q2 = A(22528, 2048)
        atT = [A(24576 + i * 512, 512) for i in range(4)]
        atM = [A(26624 + i * 512, 512) for i in range(4)]
        cst = [A(28672 + i * 512, 512) for i in range(2)]
        dwin = [A(29696 + i * 2048, 2048) for i in range(2)]
        dwout = A(33792, 2048)
        k2b = A(35840, 2048)

        rs_row = rows_f[:, 0:512]
        rs_bf = rows_bf[:, 0:512]

        wait("pool", cDVE, v_xn_all_d)
        dma("pool", SM["dCOL"], bc1[:], ds_row.partition_broadcast(128).squeeze(1))
        v_ds = SM["dCOL"].n

        cache_jobs = []
        for isup in range(NSUP):
            for hb in range(2):
                r0, r1 = hb * 1024, (hb + 1) * 1024
                cache_jobs.append((guc[r0:r1, isup * 1024:isup * 1024 + 512],
                                   gwT[r0:r1, isup * 512:(isup + 1) * 512]))
                cache_jobs.append((guc[r0:r1, isup * 1024 + 512:isup * 1024 + 1024],
                                   uwT[r0:r1, isup * 512:(isup + 1) * 512]))
        for c8 in range(8):
            cache_jobs.append((woc[c8 * 256:(c8 + 1) * 256, :], woT[c8 * 256:(c8 + 1) * 256, :]))
        cache_state = [0]

        def cache_step(n):
            for _ in range(n):
                if cache_state[0] >= len(cache_jobs):
                    return
                o, i = cache_jobs[cache_state[0]]
                dma("pool", SM["dWC"], o, i)
                cache_state[0] += 1

        dwc_state = {"in": 0, "fold": 0}
        v_din = [0] * IC
        v_fold = [0] * IC

        def dwc_issue_in(n):
            for _ in range(n):
                i = dwc_state["in"]
                if i >= IC:
                    return
                slot = i % 2
                if i >= 2:
                    wait("pool", cDVE, v_fold[i - 2])
                v_din[i] = dma("pool", SM["dDI%d" % slot], dwin[slot][:], dwT[i * 128:(i + 1) * 128, :])
                dwc_state["in"] += 1

        def dwc_fold(n):
            for _ in range(n):
                i = dwc_state["fold"]
                if i >= dwc_state["in"] or i >= IC:
                    return
                slot = i % 2
                wait("dve", SM["dDI%d" % slot], v_din[i])
                wait("dve", SM["dCOL"], v_ds)
                if i >= 1:
                    wait("dve", SM["dDO"], i * 16)
                v_fold[i] = op("dve", cDVE, lambda e, slot=slot: e.tensor_tensor(
                    dwout[:], dwin[slot][:], bc1[:], ALU.mult))
                wait("pool", cDVE, v_fold[i])
                dma("pool", SM["dDO"], dwc[i * 128:(i + 1) * 128, :], dwout[:])
                dwc_state["fold"] += 1

        def dwc_step(n):
            for _ in range(n):
                dwc_fold(1)
                dwc_issue_in(1)

        v_wq = [0] * H
        v_wk = [0] * H
        v_vh = [0] * H
        pe_head_done = [0] * H

        def head_weights(h):
            slot = h % 3
            if h >= 3:
                wait("pool", cPE, pe_head_done[h - 3])
            v_wq[h] = dma("pool", SM["dWQ%d" % slot], wqh[slot].rearrange("p (c n) -> p c n", n=128),
                          wqT[:, h * 128:(h + 1) * 128].rearrange("(c p) n -> p c n", p=128))
            v_wk[h] = dma("pool", SM["dWK%d" % slot], wkh[slot].rearrange("p (c n) -> p c n", n=128),
                          wkT[:, h * 128:(h + 1) * 128].rearrange("(c p) n -> p c n", p=128))
            if h == 0:
                wait("pool", SM["dVS0"], v_vs0)
                wait("pool", SM["dVS1"], v_vs1)
                wait("pool", SM["dVS2"], v_vs2)
            v_vh[h] = dma("pool", SM["dVH%d" % slot], vh[slot].rearrange("p (c n) -> p c n", n=128),
                          v_scr[:, h * 128:(h + 1) * 128].rearrange("(c p) n -> p c n", p=128))

        wait("pool", cPE, v_vproj_pe)
        head_weights(0)
        head_weights(1)
        dwc_issue_in(2)

        atT_free = [None, None, None, None]
        atM_free = [None, None, None, None]
        last_ctx = {0: 0, 1: 0}
        v_negc_reader = [0, 0]

        for h in range(H):
            slot = h % 3
            nslot = h % 2
            if h + 2 < H:
                head_weights(h + 2)

            wait("pe", SM["dWQ%d" % slot], v_wq[h])
            wait("act", cDVE, v_bqs)
            v_qk_evac = []
            v_sqq = [0] * 4
            v_sqk = [0] * 4
            v_qn2 = [0]
            v_kn2 = [0]

            def proj_grp(wsb, dst, sqb, v_sq, bias_ap, tc):
                b = 6 + (tc % 2)
                pe_take(b)
                v_mm = 0
                for i in range(DC):
                    emitter = (lambda f: op("pe", cPE, f)) if i == DC - 1 else (lambda f: op_("pe", f))
                    v_mm = emitter(lambda e, wsb=wsb, i=i, tc=tc, b=b: e.matmul(
                        bank[b][:, :], wsb[:, i * 128:(i + 1) * 128],
                        xn[i][:, tc * 512:(tc + 1) * 512],
                        start=(i == 0), stop=(i == DC - 1)))
                wait("act", cPE, v_mm)
                v_ev = op("act", cACT, lambda e, dst=dst, tc=tc, b=b, bias_ap=bias_ap, h=h: e.activation(
                    dst[:, tc * 512:(tc + 1) * 512], bank[b][:, :], AF.Identity,
                    bias=bias_ap[:, h:h + 1], scale=SCALE))
                bank_rel[b] = (cACT, v_ev)
                v_qk_evac.append(v_ev)
                wait("act", cACT, v_ev)
                v_sq[tc] = op("act", cACT, lambda e, dst=dst, sqb=sqb, tc=tc: e.activation(
                    sqb[:, tc * 512:(tc + 1) * 512], dst[:, tc * 512:(tc + 1) * 512], AF.Square))

            def sum_mm(sqb, v_sq, bsum, sl, vacc):
                if sl == 0:
                    pe_take(bsum)
                wait("pe", cACT, v_sq[sl])
                emitter = (lambda f: op("pe", cPE, f)) if sl == 3 else (lambda f: op_("pe", f))
                vacc[0] = emitter(lambda e, sqb=sqb, bsum=bsum, sl=sl: e.matmul(
                    bank[bsum][0:1, :], ones_col[:], sqb[:, sl * 512:(sl + 1) * 512],
                    start=(sl == 0), stop=(sl == 3)))

            for tc in range(4):
                proj_grp(wqh[slot], Qh, q2, v_sqq, bqs, tc)
                if tc >= 1:
                    sum_mm(q2, v_sqq, 0, tc - 1, v_qn2)
            dwc_step(2)
            cache_step(3)
            wait("pe", SM["dWK%d" % slot], v_wk[h])
            for tc in range(4):
                proj_grp(wkh[slot], Kh, k2b, v_sqk, bks, tc)
                if tc == 0:
                    sum_mm(q2, v_sqq, 0, 3, v_qn2)
                else:
                    sum_mm(k2b, v_sqk, 1, tc - 1, v_kn2)
            wait("dve", cPE, v_qn2[0])
            v_qm = op("dve", cDVE, lambda e: e.reduce_max(ktmp[:, 4:5], bank[0][0:1, :], axis=AX.X))
            bank_rel[0] = (cDVE, v_qm)

            pre_sc = {}
            for j in range(3):
                b_s = [4, 5, 6][j % 3]
                pe_take(b_s)
                pre_sc[(0, j)] = op("pe", cPE, lambda e, b_s=b_s, j=j: e.matmul(
                    bank[b_s][:, :], Kh[:, j * 128:(j + 1) * 128],
                    Qh[:, 0:512], start=True, stop=True))
            sum_mm(k2b, v_sqk, 1, 3, v_kn2)
            wait("dve", cPE, v_kn2[0])
            v_km = op("dve", cDVE, lambda e: e.reduce_max(ktmp[:, 12:13], bank[1][0:1, :], axis=AX.X))
            bank_rel[1] = (cDVE, v_km)
            wait("dve", cDVE, v_km)
            v_negc_row = op("dve", cDVE, lambda e: e.tensor_tensor(ktmp[:, 6:7], ktmp[:, 4:5], ktmp[:, 12:13], ALU.add))
            pe_take(7)
            wait("pe", cDVE, v_negc_row)
            v_ncmm = op("pe", cPE, lambda e: e.matmul(
                bank[7][:, 0:1], halfneg_row[:], ktmp[:, 6:7], start=True, stop=True))
            wait("dve", cPE, v_ncmm)
            wait("dve", cACT, v_negc_reader[nslot])
            v_negc = op("dve", cDVE, lambda e, nslot=nslot: e.tensor_copy(negc[nslot][:], bank[7][:, 0:1]))
            bank_rel[7] = (cDVE, v_negc)
            wait("act", cDVE, v_negc)

            wait("pe", SM["dVH%d" % slot], v_vh[h])
            wait("dve", cPOOL, v_mask)
            for qs in range(4):
                njs = 4 * qs + 4
                b_ctx = [0, 3][qs % 2]
                b_sum = [1, 2][qs % 2]
                v_exp = {}
                v_mskd = {}
                v_avsum = {}
                src_of = {}

                def emit_scores(j, qs=qs, njs=njs):
                    jp = j % 4
                    b_s = [4, 5, 6][([0, 4, 12, 24][qs] + j) % 3]
                    if (qs, j) in pre_sc:
                        v_sc = pre_sc[(qs, j)]
                    else:
                        pe_take(b_s)
                        v_sc = op("pe", cPE, lambda e, b_s=b_s, j=j, qs=qs: e.matmul(
                            bank[b_s][:, :], Kh[:, j * 128:(j + 1) * 128],
                            Qh[:, qs * 512:(qs + 1) * 512], start=True, stop=True))
                    wait("act", cPE, v_sc)
                    if atT_free[jp] is not None:
                        sem, v = atT_free[jp]
                        wait("act", sem, v)
                    v_exp[j] = op("act", cACT, lambda e, jp=jp, b_s=b_s, nslot=nslot: e.activation(
                        atT[jp][:], bank[b_s][:, :], AF.Exp, bias=negc[nslot][:]))
                    v_negc_reader[nslot] = v_exp[j]
                    bank_rel[b_s] = (cACT, v_exp[j])
                    m = j - 4 * qs
                    if m >= 0:
                        wait("dve", cACT, v_exp[j])
                        if atM_free[jp] is not None:
                            sem, v = atM_free[jp]
                            wait("dve", sem, v)
                        v_mskd[j] = op("dve", cDVE, lambda e, jp=jp, m=m: e.tensor_tensor(
                            atM[jp][:], atT[jp][:], mask[:, 384 - 128 * m: 896 - 128 * m], ALU.mult))
                        src_of[j] = atM[jp]
                    else:
                        src_of[j] = atT[jp]

                def emit_av(j, qs=qs, njs=njs, b_ctx=b_ctx, b_sum=b_sum):
                    jp = j % 4
                    src = src_of[j]
                    if j == 0:
                        pe_take(b_ctx)
                        pe_take(b_sum)
                    if j in v_mskd:
                        wait("pe", cDVE, v_mskd[j])
                    else:
                        wait("pe", cACT, v_exp[j])
                    op_("pe", lambda e, j=j, src=src, slot=slot, njs=njs, b_ctx=b_ctx: e.matmul(
                        bank[b_ctx][:, :], vh[slot][:, j * 128:(j + 1) * 128], src[:],
                        start=(j == 0), stop=(j == njs - 1)))
                    v_avsum[j] = op("pe", cPE, lambda e, j=j, src=src, njs=njs, b_sum=b_sum: e.matmul(
                        bank[b_sum][:, :], ones128[:], src[:],
                        start=(j == 0), stop=(j == njs - 1)))
                    atT_free[jp] = (cPE, v_avsum[j])
                    if j in v_mskd:
                        atM_free[jp] = (cPE, v_avsum[j])

                for j in range(njs):
                    emit_scores(j)
                    if j >= 3:
                        emit_av(j - 3)
                for j in range(max(0, njs - 3), njs):
                    emit_av(j)
                v_stripe_mm = v_avsum[njs - 1]

                sl2 = qs % 2
                wait("act", cPE, v_stripe_mm)
                op("act", cACT, lambda e, sl2=sl2, b_sum=b_sum: e.activation(
                    rs_sb[sl2][:], bank[b_sum][:, :], AF.Ln))
                wait("act", cACT, cACT.n)
                if last_ctx[1 - sl2]:
                    wait("act", cDVE, last_ctx[1 - sl2])
                v_cp = op("act", cACT, lambda e, sl2=sl2: e.activation(
                    rs_sb[1 - sl2][:], rs_sb[sl2][:], AF.Exp, scale=-1.0))
                bank_rel[b_sum] = (cACT, v_cp)
                wait("dve", cACT, v_cp)
                csl = (h * 4 + qs) % 2
                dsem = SM["dCX%d" % csl]
                wait("dve", dsem, dsem.n)
                v_ctx = op("dve", cDVE, lambda e, sl2=sl2, csl=csl, b_ctx=b_ctx: e.tensor_tensor(
                    cst[csl][:], bank[b_ctx][:, :], rs_sb[1 - sl2][:], ALU.mult))
                last_ctx[1 - sl2] = v_ctx
                bank_rel[b_ctx] = (cDVE, v_ctx)
                wait("sp", cDVE, v_ctx)
                dma("sp", dsem, ctx_scr[h * 128:(h + 1) * 128, qs * 512:(qs + 1) * 512], cst[csl][:])
            pe_head_done[h] = cPE.n
            dwc_step(1)
            cache_step(1)

        while dwc_state["fold"] < IC:
            dwc_issue_in(1)
            dwc_fold(1)
        cache_step(len(cache_jobs))
        v_dwc = SM["dDO"].n
        v_wc = SM["dWC"].n
        v_ctx_all0, v_ctx_all1 = SM["dCX0"].n, SM["dCX1"].n

        sg = [A(i * 512, 512) for i in range(IC)]
        csl_t = A(22528, 8192)
        gut = [A(30720 + i * 1024, 1024) for i in range(6)]
        dk = [A(36864 + i * 1024, 1024) for i in range(3)]
        wkt = [A(39936 + i * 1024, 1024) for i in range(3)]
        sgt = [A(43008 + i * 512, 512) for i in range(2)]

        wait("pool", cDVE, v_vev_all)
        dma("pool", SM["dCOL"], bc0[:], bo_row.partition_broadcast(128).squeeze(1))
        v_bo = SM["dCOL"].n

        full = (stage == "full")
        v_gumm = {}
        ngt = [0]
        v_dkmm = {}
        ndk = [0]
        v_womm = {}
        nwo = [0]
        v_stt = {}
        v_ep1 = {}
        v_ep2_slot = {0: 0, 1: 0}
        nep = [0]
        pe_tc_done = 0

        wait("dve", SM["dCX0"], v_ctx_all0)
        wait("dve", SM["dCX1"], v_ctx_all1)
        wait("sp", SM["dWC"], v_wc)
        wait("sp", SM["dDO"], v_dwc)
        wait("act", SM["dWC"], v_wc)
        wait("act", SM["dDO"], v_dwc)
        for tc in range(4):
            wait("pool", SM["dCX0"], v_ctx_all0)
            wait("pool", SM["dCX1"], v_ctx_all1)
            wait("pool", SM["dDO"], v_dwc)
            wait("pool", SM["dWC"], v_wc)
            if tc >= 1:
                wait("pool", cPE, pe_tc_done)
            dma("pool", SM["dCS"], csl_t.rearrange("p (c n) -> p c n", n=512),
                ctx_scr[:, tc * 512:(tc + 1) * 512].rearrange("(c p) n -> p c n", p=128))
            v_cs = SM["dCS"].n

            if full:
                for isup in range(NSUP):
                    for hh in range(DC):
                        k = ngt[0]
                        gslot = k % 6
                        gsem = SM["dGT%d" % gslot]
                        if k >= 6:
                            wait("sp", cPE, v_gumm[k - 6])
                        v_g = dma("sp", gsem, gut[gslot][:],
                                  guc[hh * 128:(hh + 1) * 128, isup * 1024:(isup + 1) * 1024])
                        wait("pe", gsem, v_g)
                        last_hh = (hh == DC - 1)
                        v_gb = [0] * 8
                        for icc in range(4):
                            if hh == 0:
                                pe_take(icc)
                            emitter = (lambda f: op("pe", cPE, f)) if last_hh else (lambda f: op_("pe", f))
                            v_gb[icc] = emitter(lambda e, gslot=gslot, icc=icc, hh=hh, tc=tc: e.matmul(
                                bank[icc][:, :], gut[gslot][:, icc * 128:(icc + 1) * 128],
                                xn[hh][:, tc * 512:(tc + 1) * 512],
                                start=(hh == 0), stop=(hh == DC - 1)))
                        for icc in range(4):
                            if hh == 0:
                                pe_take(4 + icc)
                            emitter = (lambda f: op("pe", cPE, f)) if (last_hh or icc == 3) else (lambda f: op_("pe", f))
                            v_gb[4 + icc] = emitter(lambda e, gslot=gslot, icc=icc, hh=hh, tc=tc: e.matmul(
                                bank[4 + icc][:, :], gut[gslot][:, 512 + icc * 128: 512 + (icc + 1) * 128],
                                xn[hh][:, tc * 512:(tc + 1) * 512],
                                start=(hh == 0), stop=(hh == DC - 1)))
                        v_gumm[k] = v_gb[7] if last_hh else v_gb[7]
                        ngt[0] += 1
                    for icc in range(4):
                        idx = isup * 4 + icc
                        wait("act", cPE, v_gb[icc])
                        if idx >= 2:
                            wait("act", cDVE, v_stt.get((tc, idx - 2), 0))
                        v_si = op("act", cACT, lambda e, icc=icc, idx=idx: e.activation(
                            sgt[idx % 2][:], bank[icc][:, :], AF.Silu, scale=gs_s[:, idx:idx + 1]))
                        bank_rel[icc] = (cACT, v_si)
                        wait("dve", cACT, v_si)
                        wait("dve", cPE, v_gb[4 + icc])
                        v_stt[(tc, idx)] = op("dve", cDVE, lambda e, icc=icc, idx=idx: e.scalar_tensor_tensor(
                            sg[idx][:], bank[4 + icc][:, :], us_s[:, idx:idx + 1],
                            sgt[idx % 2][:], ALU.mult, ALU.mult))
                        bank_rel[4 + icc] = (cDVE, v_stt[(tc, idx)])
                v_gu_done = v_stt[(tc, IC - 1)]

            for hhalf in range(2):
                if full:
                    wait("pe", cDVE, v_gu_done)
                    for i in range(IC):
                        k = ndk[0]
                        dslot = k % 3
                        dsem = SM["dDK%d" % dslot]
                        if k >= 3:
                            wait("sp", cPE, v_dkmm[k - 3])
                        v_d = dma("sp", dsem, dk[dslot][:],
                                  dwc[i * 128:(i + 1) * 128, hhalf * 1024:(hhalf + 1) * 1024])
                        wait("pe", dsem, v_d)
                        for ts in range(4):
                            for hc in range(2):
                                if i == 0:
                                    pe_take(ts * 2 + hc)
                                last = (ts == 3 and hc == 1)
                                emitter = (lambda f: op("pe", cPE, f)) if last else (lambda f: op_("pe", f))
                                v_dkmm[k] = emitter(lambda e, i=i, ts=ts, hc=hc, dslot=dslot: e.matmul(
                                    bank[ts * 2 + hc][:, :], sg[i][:, ts * 128:(ts + 1) * 128],
                                    dk[dslot][:, hc * 512:(hc + 1) * 512],
                                    start=(i == 0), stop=False))
                        ndk[0] += 1
                wait("pe", SM["dCS"], v_cs)
                v_ob = [0] * 8
                for dd in range(DC):
                    k = nwo[0]
                    wslot = k % 3
                    wsem = SM["dWO%d" % wslot]
                    if k >= 3:
                        wait("act", cPE, v_womm[k - 3])
                    v_w = dma("act", wsem, wkt[wslot][:],
                              woc[dd * 128:(dd + 1) * 128, hhalf * 1024:(hhalf + 1) * 1024])
                    wait("pe", wsem, v_w)
                    last_dd = (dd == DC - 1)
                    for ts in range(4):
                        for hc in range(2):
                            if not full and dd == 0:
                                pe_take(ts * 2 + hc)
                            last = (ts == 3 and hc == 1)
                            emitter = (lambda f: op("pe", cPE, f)) if (last or last_dd) else (lambda f: op_("pe", f))
                            v_ob[ts * 2 + hc] = emitter(lambda e, dd=dd, ts=ts, hc=hc, wslot=wslot: e.matmul(
                                bank[ts * 2 + hc][:, :],
                                csl_t[:, dd * 512 + ts * 128: dd * 512 + (ts + 1) * 128],
                                wkt[wslot][:, hc * 512:(hc + 1) * 512],
                                start=(not full and dd == 0), stop=(dd == DC - 1)))
                    v_womm[k] = v_ob[7]
                    nwo[0] += 1
                v_all_mm = v_womm[nwo[0] - 1]
                pe_tc_done = v_all_mm

                v_xld = {}
                ep_base = nep[0]

                def emit_xload(bidx, tc=tc, hhalf=hhalf, ep_base=ep_base):
                    ts, hc = bidx // 2, bidx % 2
                    t0 = tc * 512 + ts * 128
                    h0 = hhalf * 1024 + hc * 512
                    k = ep_base + bidx
                    xslot = k % 4
                    xsem = SM["dXT%d" % xslot]
                    if k >= 4 and (k - 4) in v_ep1:
                        wait("act", cDVE, v_ep1[k - 4])
                    v_xld[bidx] = (xsem, dma("act", xsem, x_t[xslot][:],
                                             x_tm[t0:t0 + 128, h0:h0 + 512]), xslot)

                for bidx in range(4):
                    emit_xload(bidx)

                for ts in range(4):
                    for hc in range(2):
                        bidx = ts * 2 + hc
                        t0 = tc * 512 + ts * 128
                        h0 = hhalf * 1024 + hc * 512
                        k = nep[0]
                        xsem, v_x, xslot = v_xld[bidx]
                        wait("dve", xsem, v_x)
                        wait("dve", cPE, v_ob[bidx])
                        wait("dve", SM["dCOL"], v_bo)
                        if v_ep2_slot[k % 2]:
                            wait("dve", cPOOL, v_ep2_slot[k % 2])
                        v_ep1[k] = op("dve", cDVE, lambda e, bidx=bidx, xslot=xslot, k=k: e.tensor_tensor(
                            s1_t[k % 2][:], bank[bidx][:, :], x_t[xslot][:], ALU.add))
                        bank_rel[bidx] = (cDVE, v_ep1[k])
                        osem = SM["dOT%d" % (k % 2)]
                        wait("pool", cDVE, v_ep1[k])
                        wait("pool", osem, osem.n)
                        v_ep2 = op("pool", cPOOL, lambda e, k=k, h0=h0: e.tensor_tensor(
                            out_t[k % 2][:], s1_t[k % 2][:], bc0[:, h0:h0 + 512], ALU.add))
                        wait("pool", cPOOL, v_ep2)
                        dma("pool", osem, out[t0:t0 + 128, h0:h0 + 512], out_t[k % 2][:])
                        v_ep2_slot[k % 2] = v_ep2
                        if bidx + 4 < 8:
                            emit_xload(bidx + 4)
                        nep[0] += 1

        wait("pool", SM["dOT0"], SM["dOT0"].n)
        wait("pool", SM["dOT1"], SM["dOT1"].n)
        return finish(nc, Q)


def finish(nc, Q):
    with nc.Block() as block:
        @block.tensor
        def _(e):
            for f in Q["pe"]:
                f(e)

        @block.scalar
        def _(e):
            for f in Q["act"]:
                f(e)

        @block.vector
        def _(e):
            for f in Q["dve"]:
                f(e)

        @block.gpsimd
        def _(e):
            for f in Q["pool"]:
                f(e)

        @block.sync
        def _(e):
            for f in Q["sp"]:
                f(e)
    return nc


def _prep_inputs(inputs):
    c = np.ascontiguousarray
    shared = {
        "wqT": c(inputs["L0_attn_Wq"].T), "wkT": c(inputs["L0_attn_Wk"].T),
        "wvT": c(inputs["L0_attn_Wv"].T), "woT": c(inputs["L0_attn_Wo"].T),
        "gwT": c(inputs["L0_ffn_gate_w"].T), "uwT": c(inputs["L0_ffn_up_w"].T),
        "dwT": c(inputs["L0_ffn_down_w"].T),
        "bq_col": c(np.asarray(inputs["L0_attn_bq"]).reshape(H, 128).T),
        "bk_col": c(np.asarray(inputs["L0_attn_bk"]).reshape(H, 128).T),
        "bv_row": c(np.asarray(inputs["L0_attn_bv"]).reshape(1, D)),
        "bo_row": c(np.asarray(inputs["L0_attn_bo"]).reshape(1, D)),
        "ds_row": c(np.asarray(inputs["L0_ffn_down_s"]).reshape(1, D)),
        "gs_col": c(np.asarray(inputs["L0_ffn_gate_s"]).reshape(IC, 128).T),
        "us_col": c(np.asarray(inputs["L0_ffn_up_s"]).reshape(IC, 128).T),
    }
    shared = {k: np.asarray(v, dtype=np.float32) for k, v in shared.items()}
    x = np.asarray(inputs["L0_attn_X"], dtype=np.float32)
    in_maps = []
    for b in range(8):
        xb = c(x[b])
        m = dict(shared)
        m["x_tm"] = xb
        m["x_fm"] = c(xb.T)
        in_maps.append(m)
    return in_maps


def kernel(**inputs):
    in_maps = _prep_inputs(inputs)
    nc = build_nc(STAGE)
    res = run_bass_kernel_spmd(nc, in_maps, core_ids=list(range(8)))
    return np.stack([np.asarray(r["out"]) for r in res.results], axis=0)
